# version 01
# baseline (speedup 1.0000x reference)
import contextlib
import os
import numpy as np
import ml_dtypes
import concourse.bass as bass
import concourse.mybir as mybir
from concourse.bass import IndirectOffsetOnAxis
from concourse.bass_utils import run_bass_kernel_spmd

F32 = mybir.dt.float32
BF16 = mybir.dt.bfloat16
I32 = mybir.dt.int32
U8 = mybir.dt.uint8
AF = mybir.ActivationFunctionType
ALU = mybir.AluOpType
AX = mybir.AxisListType

W = 8
TL = 2048
D = 1024
DI = 2048
DS = 16
NKD = D // 128
NBI = DI // 128

DBG = None

npf = lambda a: np.ascontiguousarray(a, dtype=np.float32)


class Grp:
    __slots__ = ("i", "val")

    def __init__(self, i):
        self.i = i
        self.val = 0


class Ctx:
    NPOOL = 44

    def __init__(self, nc, stack):
        self.nc = nc
        self.sems = {}
        self.cnt = {}
        for name in ("SC", "SV", "SS", "ST", "SP"):
            self.sems[name] = stack.enter_context(nc.semaphore(name))
            self.cnt[name] = 0
        self.pool = [stack.enter_context(nc.semaphore(f"D{i}"))
                     for i in range(self.NPOOL)]
        self.pcum = [0] * self.NPOOL
        self.pnext = 0

    def inc(self, inst, s):
        inst.then_inc(self.sems[s], 1)
        self.cnt[s] += 1
        return self.cnt[s]

    def w(self, eng, s, v):
        eng.wait_ge(self.sems[s], v)

    def wall(self, eng, *pairs):
        for s, v in pairs:
            eng.wait_ge(self.sems[s], v)

    def wcur(self, eng, *names):
        for s in names:
            eng.wait_ge(self.sems[s], self.cnt[s])

    def edge(self, eng, inst, s):
        self.inc(inst, s)
        eng.wait_ge(self.sems[s], self.cnt[s])

    def group(self):
        i = self.pnext % self.NPOOL
        self.pnext += 1
        g = Grp(i)
        g.val = self.pcum[i]
        return g

    def gdma(self, q, out, in_, g):
        d = q.dma_start(out=out, in_=in_)
        d.then_inc(self.pool[g.i], 16)
        self.pcum[g.i] += 16
        g.val = self.pcum[g.i]
        return d

    def gidma(self, q, out, in_, idx, g, eoff=0):
        d = q.indirect_dma_start(out=out, out_offset=None, in_=in_,
                                 in_offset=IndirectOffsetOnAxis(idx, 0),
                                 element_offset=eoff)
        d.then_inc(self.pool[g.i], 16)
        self.pcum[g.i] += 16
        g.val = self.pcum[g.i]
        return d

    def gw(self, eng, *groups):
        for g in groups:
            eng.wait_ge(self.pool[g.i], g.val)

    def gw_all_dma(self, eng):
        for i in range(self.NPOOL):
            if self.pcum[i]:
                eng.wait_ge(self.pool[i], self.pcum[i])


class Arena:
    def __init__(self, nc, name, words):
        self.t = nc.alloc_sbuf_tensor(name, [128, words], F32)
        self.words = words
        self.top = 0
        self.hwm = 0

    def reset(self, top=0):
        self.top = top

    def alloc(self, nrows, ncols, dtype):
        esz = {F32: 4, BF16: 2, I32: 4, U8: 1}[dtype]
        nbytes = (ncols * esz + 31) & ~31
        off = self.top
        self.top += nbytes
        self.hwm = max(self.hwm, self.top)
        assert self.top <= self.words * 4, (
            f"arena overflow: {self.top} > {self.words * 4}")
        ap = self.t.ap()
        if dtype != F32:
            ap = ap.bitcast(dtype)
        view = ap[:, off // esz: off // esz + ncols]
        if nrows != 128:
            view = view[0:nrows, :]
        return view


def build_program():
    nc = bass.Bass("TRN2", target_bir_lowering=False)
    stack = contextlib.ExitStack()
    cx = Ctx(nc, stack)

    def din(name, shape, dtype=F32):
        return nc.dram_tensor(name, shape, dtype, kind="ExternalInput").ap()

    dbg_copies = []
    PHASES = ["a1", "a2", "scan", "c", "ffn1", "attnb", "full"]

    def dint(name, shape, dtype=F32, dbg=None):
        t = nc.dram_tensor(name, shape, dtype, kind="Internal").ap()
        if DBG and dbg is not None and                 PHASES.index(dbg) <= PHASES.index(DBG):
            o = nc.dram_tensor("dbg_" + name, shape, dtype,
                               kind="ExternalOutput").ap()
            dbg_copies.append((o, t))
        return t

    x_in = din("x", [TL, D])
    xh3 = din("xh3", [128, D])
    w_in = din("w_in", [D, 4 * D])
    b_in_c = din("b_in_c", [128, 32])
    b_in_hc = din("b_in_hc", [128, 16])
    conv_w_c = din("conv_w_c", [128, 64])
    conv_b_c = din("conv_b_c", [128, 16])
    w_xp = din("w_xp", [DI, 32])
    b_xp_c = din("b_xp_c", [32, 1])
    w_dt = din("w_dt", [DI, DI])
    b_dt_c = din("b_dt_c", [128, 16])
    a_cols = din("a_cols", [128, 32])
    d_skip_c = din("d_skip_c", [128, 16])
    w_out = din("w_out", [DI, D])
    b_out_r = din("b_out_r", [1, D])
    alpha_r = din("alpha_r", [1, D])
    beta_r = din("beta_r", [1, D])
    wq = din("wq", [D, D])
    bq_c = din("bq_c", [128, 8])
    wk = din("wk", [D, 256])
    bk_c = din("bk_c", [128, 2])
    wv = din("wv", [D, 256])
    bv_c = din("bv_c", [128, 2])
    wc_i = din("wc", [256, 64])
    we_i = din("we", [64, 256])
    wo = din("wo", [D, D])
    bo_r = din("bo_r", [1, D])
    gamma_r = din("gamma_r", [1, D])
    beta_b_r = din("beta_b_r", [1, D])
    fw = din("fw", [D, D])
    fb_r = din("fb_r", [1, D])
    ident = din("ident", [128, 128])
    trimask = din("trimask", [128, 128])
    ones_r = din("ones_r", [1, 128])
    patsel = din("patsel", [16, 16 * 128])
    epat = din("epat", [128, 8 * 128])
    pmask = din("pmask", [64, 8])
    cconst = din("cconst", [128, 8])
    gidx_du = din("gidx_du", [128, 16], I32)
    gidx_y = din("gidx_y", [128, 16], I32)

    out_t = nc.dram_tensor("out", [TL, D], F32, kind="ExternalOutput").ap()

    xres = dint("xres", [TL, D], dbg="a1")
    sp_sz = dint("sp_sz", [DI, TL], dbg="a1")
    sp_p1 = dint("sp_p1", [DI, TL])
    sp_xc = dint("sp_xc", [DI, TL], dbg="a1")
    agd_in = dint("agd_in", [DI, TL], dbg="a2")
    agu_in = dint("agu_in", [DI, TL], dbg="a2")
    agbc_in = dint("agbc_in", [32, TL], dbg="a2")
    agd_out = dint("agd_out", [W * DI, TL])
    agu_out = dint("agu_out", [W * DI, TL])
    agbc_out = dint("agbc_out", [W * 32, TL])
    agy_in = dint("agy_in", [TL, TL], dbg="scan")
    agy_out = dint("agy_out", [W * TL, TL])
    ags_in = dint("ags_in", [64, 264])
    ags_out = dint("ags_out", [W * 64, 264])
    rowscr = dint("rowscr", [1, 512])

    RG = [list(range(W))]

    A = Arena(nc, "ARENA", 33280)
    P = Arena(nc, "PERS", 19800)

    identb = P.alloc(128, 128, BF16)
    trimaskb = P.alloc(128, 128, BF16)
    ones_bf = P.alloc(1, 128, BF16)
    patselb = P.alloc(16, 16 * 128, BF16)
    epatb = P.alloc(128, 8 * 128, BF16)
    sb_binc = P.alloc(128, 32, F32)
    sb_binhc = P.alloc(128, 16, F32)
    sb_convw = P.alloc(128, 64, F32)
    sb_convb = P.alloc(128, 16, F32)
    sb_bdt = P.alloc(128, 16, F32)
    sb_acol_raw = P.alloc(128, 32, F32)
    sb_acol = P.alloc(128, 32, F32)
    sb_dskip = P.alloc(128, 16, F32)
    sb_bxp = P.alloc(32, 1, F32)
    sb_bqc = P.alloc(128, 8, F32)
    sb_bkc = P.alloc(128, 2, F32)
    sb_bvc = P.alloc(128, 2, F32)
    sb_pmask = P.alloc(64, 8, F32)
    sb_cconst = P.alloc(128, 8, F32)
    sb_gidx_du = P.alloc(128, 16, I32)
    sb_gidx_y = P.alloc(128, 16, I32)
    st_mu = P.alloc(128, 16, F32)
    st_r = P.alloc(128, 16, F32)
    st_nb = P.alloc(128, 16, F32)
    st_sq = P.alloc(128, 16, F32)
    st_s1 = P.alloc(128, 16, F32)
    st_tmp = P.alloc(128, 16, F32)
    hcarry = P.alloc(128, 32, F32)
    diagw = P.alloc(128, 64 * 128, BF16)
    alpha_rep = P.alloc(128, D, F32)
    beta_rep = P.alloc(128, D, F32)
    gamma_rep = P.alloc(128, D, F32)
    betab_rep = P.alloc(128, D, F32)
    wdT_bf = P.alloc(128, 8 * D, BF16)
    wkc_bf = P.alloc(128, 8 * 256, BF16)
    wvc_bf = P.alloc(128, 8 * 256, BF16)
    sb_bk2 = P.alloc(128, 2, F32)
    sb_bv2r = P.alloc(1, 256, F32)
    rowst = P.alloc(1, D, F32)

    PA = nc.alloc_psum_tensor("PA", [128, 1024], F32)
    PB = nc.alloc_psum_tensor("PB", [128, 1024], F32)
    PC = nc.alloc_psum_tensor("PC", [128, 1024], F32)
    PD = nc.alloc_psum_tensor("PD", [128, 512], F32)
    PE = nc.alloc_psum_tensor("PE", [128, 1024], BF16)

    CONSTS = [1e-5, 1e-6, 1.0, -1.0, 0.5, -0.5, 0.0, 2.0]

    a1_winbf = A.alloc(128, 8 * 4096, BF16)

    A1MAX = int(os.environ.get("A1MAX", "8"))

    block = stack.enter_context(nc.Block())

    for ci, cv in enumerate(CONSTS):
        nc.const_aps.register(F32, cv, sb_cconst[:, ci:ci + 1])

    stopped = [False]

    def maybe_stop(tag):
        if DBG != tag or stopped[0]:
            return False
        stopped[0] = True

        @block.sync
        def _(q):
            cx.wcur(q, "SS", "SV", "ST", "SP", "SC")
            cx.gw_all_dma(q)
            g = cx.group()
            for o, t in dbg_copies:
                cx.gdma(q, o, t, g)
            cx.gdma(q, out_t, x_in, g)
            cx.gw(q, g)
        return True

    g_c = cx.group()
    g_cb = cx.group()
    g_rep = cx.group()

    @block.sync
    def _(q):
        for dst, s_ in [(sb_binc, b_in_c), (sb_binhc, b_in_hc),
                        (sb_convw, conv_w_c),
                        (sb_convb, conv_b_c), (sb_bdt, b_dt_c),
                        (sb_acol_raw, a_cols), (sb_dskip, d_skip_c),
                        (sb_bxp, b_xp_c), (sb_bqc, bq_c), (sb_bkc, bk_c),
                        (sb_bvc, bv_c), (sb_pmask, pmask),
                        (sb_gidx_du, gidx_du), (sb_gidx_y, gidx_y),
                        (sb_cconst, cconst)]:
            cx.gdma(q, dst, s_, g_c)
        for dst, s_ in [(alpha_rep, alpha_r), (beta_rep, beta_r),
                        (gamma_rep, gamma_r), (betab_rep, beta_b_r)]:
            cx.gdma(q, dst, s_.broadcast_to([128, D]), g_rep)
        cx.gdma(q, xres, x_in, g_rep)

    @block.gpsimd
    def _(q):
        for dst, s_ in [(identb, ident), (trimaskb, trimask),
                        (ones_bf, ones_r), (patselb, patsel),
                        (epatb, epat)]:
            cx.gdma(q, dst, s_, g_cb)
        for k in range(NKD):
            cx.gdma(q, a1_winbf[:, k * 4096:(k + 1) * 4096],
                    w_in[k * 128:(k + 1) * 128, :], g_cb)

    @block.scalar
    def _(s):
        cx.gw(s, g_c)
        i1 = s.activation(sb_acol, sb_acol_raw, AF.Exp)
        cx.edge(s, i1, "SS")
        cx.inc(s.activation(sb_acol, sb_acol, AF.Identity, scale=-1.0), "SS")
    p0_ss = cx.cnt["SS"]

    @block.vector
    def _(v):
        cx.gw(v, g_c, g_cb)
        mm = None
        for blk in range(16):
            for kk in range(4):
                mm = v.tensor_scalar_mul(
                    diagw[:, (blk * 4 + kk) * 128:(blk * 4 + kk + 1) * 128],
                    identb, sb_convw[:, blk * 4 + kk:blk * 4 + kk + 1])
        cx.inc(mm, "SV")
    p0_sv = cx.cnt["SV"]

    A.reset(A.top)
    a0_x = A.alloc(128, D, F32)
    a0_x2 = A.alloc(128, D, F32)
    a0_sq = A.alloc(128, D, F32)
    g_a0 = [cx.group(), cx.group()]

    for i in range(16):
        xa = a0_x if i % 2 == 0 else a0_x2
        g = g_a0[i % 2]

        @block.sync
        def _(q):
            if i >= 2:
                cx.wall(q, ("SS", p0_ss + i - 1), ("SV", p0_sv + i - 1))
            cx.gdma(q, xa, x_in[i * 128:(i + 1) * 128, :], g)

        @block.scalar
        def _(s):
            cx.gw(s, g)
            cx.wcur(s, "SS")
            cx.inc(s.activation(a0_sq, xa, AF.Square,
                                accum_out=st_sq[:, i:i + 1]), "SS")

        @block.vector
        def _(v):
            cx.gw(v, g)
            cx.inc(v.reduce_sum(st_s1[:, i:i + 1], xa, AX.X), "SV")

    def ln_stats_tail(r_dst, nb_dst, mu_dst, sq_src, s1_src, cols=16):
        @block.vector
        def _(v):
            cx.wcur(v, "SS", "SV")
            i1 = v.tensor_scalar_mul(mu_dst, s1_src, 1.0 / D)
            cx.edge(v, i1, "SV")
            i2 = v.tensor_mul(st_tmp[:, 0:cols], mu_dst, mu_dst)
            cx.edge(v, i2, "SV")
            cx.inc(v.scalar_tensor_tensor(st_tmp[:, 0:cols], sq_src, 1.0 / D,
                                          st_tmp[:, 0:cols], ALU.mult,
                                          ALU.subtract), "SV")
        m_var = cx.cnt["SV"]

        @block.scalar
        def _(s):
            cx.w(s, "SV", m_var)
            i1 = s.activation(r_dst, st_tmp[:, 0:cols], AF.Ln, bias=1e-5)
            cx.edge(s, i1, "SS")
            cx.inc(s.activation(r_dst, r_dst, AF.Exp, scale=-0.5), "SS")
        m_r = cx.cnt["SS"]

        @block.vector
        def _(v):
            cx.w(v, "SS", m_r)
            cx.inc(v.scalar_tensor_tensor(nb_dst, mu_dst, -1.0, r_dst,
                                          ALU.mult, ALU.mult), "SV")
        return cx.cnt["SV"]

    m_nb = ln_stats_tail(st_r, st_nb, st_mu, st_sq, st_s1)

    A.reset(65536)
    a1_x = A.alloc(128, D, F32)
    a1_xhb = A.alloc(128, D, BF16)
    a1_xhT = A.alloc(128, 2 * D, BF16)
    a1_xin = A.alloc(128, 16 * 259, BF16)
    a1_sz = A.alloc(128, 16 * 256, F32)
    a1_xc = A.alloc(128, 16 * 256, F32)
    a1_sk = A.alloc(128, 256, F32)
    a1_h3T = A.alloc(128, 8 * 128, BF16)
    a1_h3 = a1_x
    a1_h3b = a1_xhb

    g_h3 = cx.group()

    @block.sync
    def _(q):
        cx.wcur(q, "SS", "SV")
        cx.gdma(q, a1_h3, xh3, g_h3)

    @block.scalar
    def _(s):
        cx.gw(s, g_h3)
        cx.inc(s.activation(a1_xin[:, 0:D], a1_h3, AF.Square,
                            accum_out=st_tmp[:, 8:9]), "SS")
    m_h3sq = cx.cnt["SS"]

    @block.vector
    def _(v):
        cx.w(v, "SS", m_h3sq)
        i1 = v.reduce_sum(st_tmp[:, 9:10], a1_h3, AX.X)
        cx.edge(v, i1, "SV")
        i2 = v.tensor_scalar_mul(st_tmp[:, 10:11], st_tmp[:, 9:10], 1.0 / D)
        cx.edge(v, i2, "SV")
        i3 = v.tensor_mul(st_tmp[:, 11:12], st_tmp[:, 10:11],
                          st_tmp[:, 10:11])
        cx.edge(v, i3, "SV")
        cx.inc(v.scalar_tensor_tensor(st_tmp[:, 11:12], st_tmp[:, 8:9],
                                      1.0 / D, st_tmp[:, 11:12],
                                      ALU.mult, ALU.subtract), "SV")
    m_h3v = cx.cnt["SV"]

    @block.scalar
    def _(s):
        cx.w(s, "SV", m_h3v)
        i1 = s.activation(st_tmp[:, 12:13], st_tmp[:, 11:12], AF.Ln,
                          bias=1e-5)
        cx.edge(s, i1, "SS")
        cx.inc(s.activation(st_tmp[:, 12:13], st_tmp[:, 12:13], AF.Exp,
                            scale=-0.5), "SS")
    m_h3r = cx.cnt["SS"]

    @block.vector
    def _(v):
        cx.w(v, "SS", m_h3r)
        cx.inc(v.scalar_tensor_tensor(st_tmp[:, 13:14], st_tmp[:, 10:11],
                                      -1.0, st_tmp[:, 12:13],
                                      ALU.mult, ALU.mult), "SV")
    m_h3nb = cx.cnt["SV"]

    @block.scalar
    def _(s):
        cx.w(s, "SV", m_h3nb)
        cx.inc(s.activation(a1_h3b, a1_h3, AF.Identity,
                            scale=st_tmp[:, 12:13],
                            bias=st_tmp[:, 13:14]), "SS")
    m_h3hat = cx.cnt["SS"]

    @block.tensor
    def _(t):
        cx.wall(t, ("SS", m_h3hat))
        cx.gw(t, g_cb)
        mm = None
        for k in range(NKD):
            mm = t.transpose(PE[:, k * 128:(k + 1) * 128],
                             a1_h3b[:, k * 128:(k + 1) * 128], identb)
        cx.inc(mm, "ST")
    m_h3t = cx.cnt["ST"]

    @block.vector
    def _(v):
        cx.w(v, "ST", m_h3t)
        cx.inc(v.tensor_copy(a1_h3T, PE[:, :]), "SV")
    m_h3T = cx.cnt["SV"]

    @block.tensor
    def _(t):
        cx.w(t, "SV", m_h3T)
        mm = None
        for blk in range(16):
            for k in range(NKD):
                mm = t.matmul(PD[:, blk * 3:blk * 3 + 3],
                              a1_winbf[:, k * 4096 + blk * 128:
                                       k * 4096 + (blk + 1) * 128],
                              a1_h3T[:, k * 128:k * 128 + 3],
                              start=(k == 0), stop=(k == NKD - 1))
        cx.inc(mm, "ST")
    m_h3mm = cx.cnt["ST"]

    @block.vector
    def _(v):
        cx.w(v, "ST", m_h3mm)
        mm = None
        for blk in range(16):
            mm = v.tensor_scalar_add(a1_xin[:, blk * 259:blk * 259 + 3],
                                     PD[:, blk * 3:blk * 3 + 3],
                                     sb_binhc[:, blk:blk + 1])
        cx.inc(mm, "SV")
    m_h3fill = cx.cnt["SV"]

    if DBG:
        xhat_d = nc.dram_tensor("dbg_xhat", [TL, D], BF16,
                                kind="ExternalOutput").ap()

    g_x = cx.group()
    g_sz = cx.group()
    g_xcs = cx.group()
    g_p1 = cx.group()

    for it in range(min(8, A1MAX)):
        t0 = it * 256
        for sub in range(2):
            gi = it * 2 + sub

            @block.sync
            def _(q):
                cx.w(q, "SV", m_h3fill)
                cx.wcur(q, "SS")
                cx.gdma(q, a1_x, x_in[gi * 128:(gi + 1) * 128, :], g_x)

            @block.scalar
            def _(s):
                cx.gw(s, g_x)
                cx.wall(s, ("SV", m_nb))
                cx.wcur(s, "ST")
                cx.inc(s.activation(a1_xhb, a1_x, AF.Identity,
                                    scale=st_r[:, gi:gi + 1],
                                    bias=st_nb[:, gi:gi + 1]), "SS")
            m_hat = cx.cnt["SS"]

            if DBG:
                @block.sync
                def _(q):
                    cx.w(q, "SS", m_hat)
                    cx.gdma(q, xhat_d[gi * 128:(gi + 1) * 128, :],
                            a1_xhb, g_x)

            @block.tensor
            def _(t):
                cx.w(t, "SS", m_hat)
                cx.wcur(t, "SV")
                mm = None
                for k in range(NKD):
                    mm = t.transpose(PE[:, k * 128:(k + 1) * 128],
                                     a1_xhb[:, k * 128:(k + 1) * 128],
                                     identb)
                cx.inc(mm, "ST")
            m_tr = cx.cnt["ST"]

            @block.vector
            def _(v):
                cx.w(v, "ST", m_tr)
                mm = None
                for k in range(NKD):
                    mm = v.tensor_copy(a1_xhT[:, k * 256 + sub * 128:
                                              k * 256 + sub * 128 + 128],
                                       PE[:, k * 128:(k + 1) * 128])
                cx.inc(mm, "SV")
        m_xhT = cx.cnt["SV"]

        if it > 0:
            @block.vector
            def _(v):
                srcv = a1_xin[:, :].rearrange("p (b c) -> p b c", c=259)
                cx.inc(v.tensor_copy(srcv[:, :, 0:3], srcv[:, :, 256:259]),
                       "SV")
        m_halo = cx.cnt["SV"]

        for po in range(32):
            pslot = PA[:, 0:256] if po % 2 == 0 else PA[:, 512:768]

            @block.tensor
            def _(t):
                cx.wall(t, ("SV", m_xhT))
                if po >= 2:
                    cx.wcur(t, "SS", "SV")
                mm = None
                for k in range(NKD):
                    mm = t.matmul(pslot,
                                  a1_winbf[:, k * 4096 + po * 128:
                                           k * 4096 + (po + 1) * 128],
                                  a1_xhT[:, k * 256:(k + 1) * 256],
                                  start=(k == 0), stop=(k == NKD - 1))
                cx.inc(mm, "ST")
            m_mm = cx.cnt["ST"]

            if po < 16:
                @block.vector
                def _(v):
                    cx.wall(v, ("ST", m_mm), ("SV", m_halo))
                    cx.inc(v.tensor_scalar_add(
                        a1_xin[:, po * 259 + 3:po * 259 + 259],
                        pslot, sb_binc[:, po:po + 1]), "SV")

            else:
                @block.scalar
                def _(s):
                    cx.w(s, "ST", m_mm)
                    if po == 16 and it > 0:
                        cx.gw(s, g_p1)
                    cx.inc(s.activation(
                        a1_sz[:, (po - 16) * 256:(po - 15) * 256],
                        pslot, AF.Silu, bias=sb_binc[:, po:po + 1]), "SS")

        m_xz = (cx.cnt["ST"], cx.cnt["SS"], cx.cnt["SV"])

        @block.sync
        def _(q):
            cx.wall(q, ("SS", m_xz[1]), ("SV", m_xz[2]))
            dd = sp_sz.rearrange("(b p) t -> p b t", p=128)
            cx.gdma(q, dd[:, :, t0:t0 + 256],
                    a1_sz.rearrange("p (b c) -> p b c", c=256), g_sz)

        for blk in range(16):
            cslot = PB[:, 0:256] if blk % 2 == 0 else PB[:, 512:768]

            @block.tensor
            def _(t):
                cx.wall(t, ("SV", m_xz[2]), ("SS", m_xz[1]))
                if blk >= 2:
                    cx.wcur(t, "SS")
                mm = None
                for kk in range(4):
                    mm = t.matmul(cslot,
                                  diagw[:, (blk * 4 + kk) * 128:
                                        (blk * 4 + kk + 1) * 128],
                                  a1_xin[:, blk * 259 + kk:
                                         blk * 259 + kk + 256],
                                  start=(kk == 0), stop=(kk == 3))
                cx.inc(mm, "ST")
            m_cv = cx.cnt["ST"]

            @block.scalar
            def _(s):
                cx.w(s, "ST", m_cv)
                if blk == 0 and it > 0:
                    cx.gw(s, g_xcs)
                cx.inc(s.activation(a1_xc[:, blk * 256:(blk + 1) * 256],
                                    cslot, AF.Silu,
                                    bias=sb_convb[:, blk:blk + 1]), "SS")
        m_xcs_c = cx.cnt["SS"]

        @block.sync
        def _(q):
            cx.w(q, "SS", m_xcs_c)
            dd = sp_xc.rearrange("(b p) t -> p b t", p=128)
            cx.gdma(q, dd[:, :, t0:t0 + 256],
                    a1_xc.rearrange("p (b c) -> p b c", c=256), g_xcs)

        for blk in range(16):
            @block.vector
            def _(v):
                if blk == 0:
                    cx.wall(v, ("SS", m_xcs_c))
                    cx.gw(v, g_sz)
                cx.wcur(v, "SV")
                i1 = v.tensor_scalar_mul(
                    a1_sk, a1_xc[:, blk * 256:(blk + 1) * 256],
                    sb_dskip[:, blk:blk + 1])
                cx.edge(v, i1, "SV")
                cx.inc(v.tensor_mul(a1_sz[:, blk * 256:(blk + 1) * 256],
                                    a1_sk,
                                    a1_sz[:, blk * 256:(blk + 1) * 256]),
                       "SV")
        m_p1 = cx.cnt["SV"]

        @block.sync
        def _(q):
            cx.w(q, "SV", m_p1)
            dd = sp_p1.rearrange("(b p) t -> p b t", p=128)
            cx.gdma(q, dd[:, :, t0:t0 + 256],
                    a1_sz.rearrange("p (b c) -> p b c", c=256), g_p1)

    a1_end = dict(cx.cnt)
    a1_groups = (g_x, g_sz, g_xcs, g_p1)

    if maybe_stop("a1"):
        stack.close()
        return nc

    A.reset(0)
    a2_wdt = A.alloc(128, 16 * 2048, BF16)
    a2_xc = A.alloc(128, 16 * 256, F32)
    a2_xcb = A.alloc(128, 16 * 256, BF16)
    a2_dl = A.alloc(128, 16 * 256, F32)
    a2_u = A.alloc(128, 16 * 256, F32)
    a2_t1 = A.alloc(128, 256, F32)
    a2_t2 = A.alloc(128, 256, F32)
    a2_t3 = A.alloc(128, 256, F32)
    a2_t4 = A.alloc(128, 256, F32)
    a2_bc = A.alloc(32, 256, F32)
    a2_wxp = A.alloc(128, 16 * 32, BF16)

    g_wdt = cx.group()
    g_xcl = cx.group()
    g_a2o = cx.group()

    @block.gpsimd
    def _(q):
        cx.wall(q, ("SS", a1_end["SS"]), ("SV", a1_end["SV"]),
                ("ST", a1_end["ST"]))
        cx.gw(q, *a1_groups)
        for k in range(NBI):
            cx.gdma(q, a2_wdt[:, k * 2048:(k + 1) * 2048],
                    w_dt[k * 128:(k + 1) * 128, :], g_wdt)
        for k in range(NBI):
            cx.gdma(q, a2_wxp[:, k * 32:(k + 1) * 32],
                    w_xp[k * 128:(k + 1) * 128, :], g_wdt)

    for it in range(8):
        t0 = it * 256

        @block.sync
        def _(q):
            cx.wcur(q, "SV", "SS", "ST")
            cx.gw(q, g_a2o)
            if it == 0:
                cx.gw(q, g_wdt)
            sxc = sp_xc.rearrange("(b p) t -> p b t", p=128)
            cx.gdma(q, a2_xc.rearrange("p (b c) -> p b c", c=256),
                    sxc[:, :, t0:t0 + 256], g_xcl)

        @block.vector
        def _(v):
            cx.gw(v, g_xcl)
            cx.inc(v.tensor_copy(a2_xcb, a2_xc), "SV")
        m_xcb = cx.cnt["SV"]

        for po in range(16):
            pslot = PA[:, 0:256] if po % 2 == 0 else PA[:, 512:768]

            @block.tensor
            def _(t):
                cx.wall(t, ("SV", m_xcb))
                cx.gw(t, g_wdt)
                if po >= 2:
                    cx.wcur(t, "SS", "SV")
                mm = None
                for k in range(NBI):
                    mm = t.matmul(pslot,
                                  a2_wdt[:, k * 2048 + po * 128:
                                         k * 2048 + (po + 1) * 128],
                                  a2_xcb[:, k * 256:(k + 1) * 256],
                                  start=(k == 0), stop=(k == NBI - 1))
                cx.inc(mm, "ST")
            m_dmm = cx.cnt["ST"]

            @block.scalar
            def _(s):
                cx.w(s, "ST", m_dmm)
                cx.wcur(s, "SV")
                i1 = s.activation(a2_t1, pslot, AF.Abs,
                                  bias=sb_bdt[:, po:po + 1])
                cx.edge(s, i1, "SS")
                i2 = s.activation(a2_t2, a2_t1, AF.Exp, scale=-1.0)
                cx.edge(s, i2, "SS")
                cx.inc(s.activation(a2_t3, a2_t2, AF.Ln, bias=1.0), "SS")
            m_sp = cx.cnt["SS"]

            @block.vector
            def _(v):
                cx.w(v, "ST", m_dmm)
                cx.wcur(v, "SV")
                i1 = v.tensor_scalar(a2_t4, pslot, sb_bdt[:, po:po + 1],
                                     0.0, ALU.add, ALU.max)
                cx.edge(v, i1, "SV")
                cx.w(v, "SS", m_sp)
                cx.inc(v.tensor_add(a2_dl[:, po * 256:(po + 1) * 256],
                                    a2_t4, a2_t3), "SV")
        m_dl = (cx.cnt["ST"], cx.cnt["SS"], cx.cnt["SV"])

        @block.tensor
        def _(t):
            cx.w(t, "SV", m_xcb)
            mm = None
            for k in range(NBI):
                mm = t.matmul(PD[0:32, 0:256],
                              a2_wxp[:, k * 32:(k + 1) * 32],
                              a2_xcb[:, k * 256:(k + 1) * 256],
                              start=(k == 0), stop=(k == NBI - 1))
            cx.inc(mm, "ST")
        m_bcm = cx.cnt["ST"]

        @block.scalar
        def _(s):
            cx.w(s, "ST", m_bcm)
            cx.inc(s.activation(a2_bc, PD[0:32, 0:256], AF.Identity,
                                bias=sb_bxp), "SS")
        m_bce = cx.cnt["SS"]

        @block.vector
        def _(v):
            cx.w(v, "SV", m_dl[2])
            cx.inc(v.tensor_mul(a2_u, a2_dl, a2_xc), "SV")
        m_u = cx.cnt["SV"]

        @block.sync
        def _(q):
            cx.wall(q, ("SV", m_u), ("SS", m_bce))
            dd = agd_in.rearrange("(b p) t -> p b t", p=128)
            cx.gdma(q, dd[:, :, t0:t0 + 256],
                    a2_dl.rearrange("p (b c) -> p b c", c=256), g_a2o)
            du = agu_in.rearrange("(b p) t -> p b t", p=128)
            cx.gdma(q, du[:, :, t0:t0 + 256],
                    a2_u.rearrange("p (b c) -> p b c", c=256), g_a2o)
            cx.gdma(q, agbc_in[:, t0:t0 + 256], a2_bc, g_a2o)

    @block.gpsimd
    def _(q):
        cx.gw(q, g_a2o)
        cx.inc(q.collective_compute("AllGather", ALU.bypass,
                                    replica_groups=RG,
                                    ins=[agbc_in.opt()],
                                    outs=[agbc_out.opt()]), "SC")
        cx.inc(q.collective_compute("AllGather", ALU.bypass,
                                    replica_groups=RG,
                                    ins=[agd_in.opt()],
                                    outs=[agd_out.opt()]), "SC")
        cx.inc(q.collective_compute("AllGather", ALU.bypass,
                                    replica_groups=RG,
                                    ins=[agu_in.opt()],
                                    outs=[agu_out.opt()]), "SC")
    m_ag = cx.cnt["SC"]

    if maybe_stop("a2"):
        stack.close()
        return nc

    A.reset(0)
    sc_crep = [A.alloc(128, 1024, F32), A.alloc(128, 1024, F32)]
    sc_dg = [[A.alloc(128, 1024, F32) for _ in range(2)] for _ in range(2)]
    sc_ug = [[A.alloc(128, 1024, F32) for _ in range(2)] for _ in range(2)]
    sc_brep = [A.alloc(128, 1024, F32), A.alloc(128, 1024, F32)]
    sc_a = [A.alloc(128, 1024, F32), A.alloc(128, 1024, F32)]
    sc_b = [A.alloc(128, 1024, F32), A.alloc(128, 1024, F32)]
    sc_h = [A.alloc(128, 1024, F32), A.alloc(128, 1024, F32)]
    sc_w = [A.alloc(128, 1024, BF16), A.alloc(128, 1024, BF16)]
    sc_y = [A.alloc(128, 1024, F32), A.alloc(128, 1024, F32)]
    g_cr = [cx.group(), cx.group()]
    g_du = [cx.group(), cx.group()]
    g_br = [cx.group(), cx.group()]
    g_y = cx.group()

    @block.vector
    def _(v):
        cx.wcur(v, "SV")
        cx.inc(v.memset(hcarry, 0.0), "SV")
    m_hc0 = cx.cnt["SV"]

    bap = agbc_out[:, :].rearrange("(j s) t -> j s t", s=32)

    agd_flat = agd_out.rearrange("r (h c) -> (r h) c", c=1024)
    agu_flat = agu_out.rearrange("r (h c) -> (r h) c", c=1024)

    def issue_gathers(q, jc, par):
        jsrc, jhalf = jc // 2, jc % 2
        cof = jhalf * 1024
        for b in range(2):
            idx = sb_gidx_du[:, b * 8 + jsrc:b * 8 + jsrc + 1]
            cx.gidma(q, sc_dg[b][par], agd_flat, idx, g_du[par], eoff=cof)
            cx.gidma(q, sc_ug[b][par], agu_flat, idx, g_du[par], eoff=cof)

    @block.gpsimd
    def _(q):
        cx.w(q, "SC", m_ag)
        cx.wcur(q, "SV", "SS", "ST", "SP")
        issue_gathers(q, 0, 0)

    if DBG:
        dbg_dg = nc.dram_tensor("dbg_dg", [128, 1024], F32,
                                kind="ExternalOutput").ap()

        @block.sync
        def _(q):
            g = cx.group()
            cx.gw(q, g_du[0])
            cx.gdma(q, dbg_dg, sc_dg[0][0], g)

    for jc in range(16):
        jsrc, jhalf = jc // 2, jc % 2
        pj = jc % 2
        if jc + 1 < 16:
            @block.gpsimd
            def _(q):
                if jc >= 1:
                    cx.wcur(q, "SS", "SV")
                issue_gathers(q, jc + 1, (jc + 1) % 2)

        for s in range(16):
            sp = s % 2

            @block.sync
            def _(q):
                if jc == 0:
                    cx.w(q, "SC", m_ag)
                if jc > 0 or s >= 2:
                    cx.wcur(q, "SV")
                row = bap[jsrc, s:s + 1, jhalf * 1024:(jhalf + 1) * 1024]
                cx.gdma(q, sc_brep[sp], row.broadcast_to([128, 1024]),
                        g_br[sp])

            @block.scalar
            def _(s_):
                if jc == 0:
                    cx.w(s_, "SC", m_ag)
                cx.wcur(s_, "SP")
                crow = bap[jsrc, 16 + s:17 + s,
                           jhalf * 1024:(jhalf + 1) * 1024]
                cx.gdma(s_, sc_crep[sp], crow.broadcast_to([128, 1024]),
                        g_cr[sp])

            for b in range(2):
                acol = sb_acol[:, b * 16 + s:b * 16 + s + 1]
                hcol = hcarry[:, b * 16 + s:b * 16 + s + 1]
                PY = PB if b == 0 else PA

                @block.scalar
                def _(s_):
                    cx.gw(s_, g_du[pj])
                    cx.wcur(s_, "SV")
                    cx.inc(s_.activation(sc_a[sp], sc_dg[b][pj], AF.Exp,
                                         scale=acol), "SS")
                m_as = cx.cnt["SS"]

                @block.vector
                def _(v):
                    cx.gw(v, g_du[pj], g_br[sp])
                    if jc == 0 and s == 0 and b == 0:
                        cx.w(v, "SV", m_hc0)
                    cx.wcur(v, "SV")
                    cx.wcur(v, "SP")
                    i1 = v.tensor_mul(sc_b[sp], sc_ug[b][pj], sc_brep[sp])
                    cx.edge(v, i1, "SV")
                    cx.w(v, "SS", m_as)
                    i2 = v.tensor_tensor_scan(sc_h[sp], sc_a[sp], sc_b[sp],
                                              hcol, ALU.mult, ALU.add)
                    cx.edge(v, i2, "SV")
                    cx.inc(v.tensor_copy(hcol, sc_h[sp][:, 1023:1024]),
                           "SV")
                m_scan = cx.cnt["SV"]

                @block.gpsimd
                def _(q):
                    cx.w(q, "SV", m_scan)
                    cx.gw(q, g_cr[sp])
                    cx.wcur(q, "ST")
                    cx.inc(q.tensor_mul(sc_w[sp], sc_h[sp], sc_crep[sp]),
                           "SP")
                m_w = cx.cnt["SP"]

                @block.tensor
                def _(t):
                    cx.w(t, "SP", m_w)
                    if s == 0:
                        cx.wcur(t, "SS")
                    mm = None
                    for f in range(2):
                        mm = t.matmul(PY[:, f * 512:(f + 1) * 512],
                                      identb,
                                      sc_w[sp][:, f * 512:(f + 1) * 512],
                                      start=(s == 0), stop=(s == 15))
                    cx.inc(mm, "ST")
        m_ymm = cx.cnt["ST"]

        for b in range(2):
            PY = PB if b == 0 else PA
            yb = sc_y[b]

            @block.scalar
            def _(s_):
                cx.w(s_, "ST", m_ymm)
                cx.gw(s_, g_y)
                cx.inc(s_.activation(yb, PY[:, :], AF.Identity), "SS")
            m_ye = cx.cnt["SS"]

            @block.sync
            def _(q):
                cx.w(q, "SS", m_ye)
                r0 = jsrc * 256 + b * 128
                cx.gdma(q, agy_in[r0:r0 + 128,
                                  jhalf * 1024:(jhalf + 1) * 1024], yb, g_y)

    @block.gpsimd
    def _(q):
        cx.gw(q, g_y)
        cx.inc(q.collective_compute("AllGather", ALU.bypass,
                                    replica_groups=RG,
                                    ins=[agy_in.opt()],
                                    outs=[agy_out.opt()]), "SC")
    m_agy = cx.cnt["SC"]

    if maybe_stop("scan"):
        stack.close()
        return nc

    A.reset(0)
    c_g = A.alloc(128, 16 * 1024, BF16)
    c_y = A.alloc(128, 1024, F32)
    c_sz = A.alloc(128, 1024, F32)
    c_p1 = A.alloc(128, 1024, F32)
    c_tmp = A.alloc(128, 1024, F32)
    c_wout = A.alloc(128, 16 * 1024, BF16)
    c_x = [A.alloc(128, 1024, F32), A.alloc(128, 1024, F32)]
    c_xh = A.alloc(128, 1024, F32)
    c_v = A.alloc(128, 1024, F32)
    c_t1 = A.alloc(128, 1024, F32)
    rowbo_bf = P.alloc(1, D, BF16)

    agy_flat = agy_out.rearrange("r (h c) -> (r h) c", c=1024)

    g_wo = cx.group()
    g_yg = cx.group()
    g_szp = cx.group()
    g_xl = [cx.group(), cx.group()]
    g_xr = cx.group()

    @block.gpsimd
    def _(q):
        cx.w(q, "SC", m_agy)
        cx.wcur(q, "SV", "SS", "ST", "SP")
        for k in range(NBI):
            cx.gdma(q, c_wout[:, k * 1024:(k + 1) * 1024],
                    w_out[k * 128:(k + 1) * 128, :], g_wo)
        cx.gdma(q, rowbo_bf, b_out_r, g_wo)

    for th in range(2):
        for m in range(16):
            @block.gpsimd
            def _(q):
                cx.gw(q, g_wo)
                cx.wcur(q, "SV")
                cx.gidma(q, c_y, agy_flat,
                         sb_gidx_y[:, m:m + 1], g_yg, eoff=th * 1024)

            @block.sync
            def _(q):
                cx.wcur(q, "SV")
                cx.gdma(q, c_sz,
                        sp_sz[m * 128:(m + 1) * 128,
                              th * 1024:(th + 1) * 1024], g_szp)
                cx.gdma(q, c_p1,
                        sp_p1[m * 128:(m + 1) * 128,
                              th * 1024:(th + 1) * 1024], g_szp)

            @block.vector
            def _(v):
                cx.gw(v, g_yg, g_szp)
                if m == 0:
                    cx.wcur(v, "ST")
                i1 = v.tensor_mul(c_tmp, c_y, c_sz)
                cx.edge(v, i1, "SV")
                cx.inc(v.tensor_add(c_g[:, m * 1024:(m + 1) * 1024],
                                    c_tmp, c_p1), "SV")
        m_g = cx.cnt["SV"]

        for tpo in range(8):
            gi = th * 8 + tpo
            xl = c_x[tpo % 2]

            @block.sync
            def _(q):
                if tpo >= 2:
                    cx.wcur(q, "SV", "SS")
                cx.gdma(q, xl, x_in[gi * 128:(gi + 1) * 128, :], g_xl[tpo % 2])

            @block.tensor
            def _(t):
                cx.wall(t, ("SV", m_g))
                cx.gw(t, g_wo)
                if tpo >= 1:
                    cx.wcur(t, "SV")
                mm = None
                for k in range(NBI):
                    for f in range(2):
                        mm = t.matmul(PA[:, f * 512:(f + 1) * 512],
                                      c_g[:, k * 1024 + tpo * 128:
                                          k * 1024 + (tpo + 1) * 128],
                                      c_wout[:, k * 1024 + f * 512:
                                             k * 1024 + (f + 1) * 512],
                                      start=(k == 0), stop=False)
                for f in range(2):
                    mm = t.matmul(PA[:, f * 512:(f + 1) * 512],
                                  ones_bf,
                                  rowbo_bf[:, f * 512:(f + 1) * 512],
                                  start=False, stop=True)
                cx.inc(mm, "ST")
            m_om = cx.cnt["ST"]

            @block.scalar
            def _(s_):
                cx.gw(s_, g_xl[tpo % 2])
                cx.wcur(s_, "SV")
                cx.inc(s_.activation(c_xh, xl, AF.Identity,
                                     scale=st_r[:, gi:gi + 1],
                                     bias=st_nb[:, gi:gi + 1]), "SS")
            m_xh = cx.cnt["SS"]

            @block.vector
            def _(v):
                cx.wall(v, ("ST", m_om), ("SS", m_xh))
                cx.inc(v.tensor_add(c_v, PA[:, :], c_xh), "SV")
            m_v = cx.cnt["SV"]

            @block.scalar
            def _(s_):
                cx.w(s_, "SV", m_v)
                cx.inc(s_.activation(c_t1, c_v, AF.Square,
                                     accum_out=st_tmp[:, 0:1]), "SS")
            m_q2 = cx.cnt["SS"]

            @block.vector
            def _(v):
                cx.w(v, "SS", m_q2)
                i1 = v.reduce_sum(st_tmp[:, 1:2], c_v, AX.X)
                cx.edge(v, i1, "SV")
                i2 = v.tensor_scalar_mul(st_tmp[:, 2:3], st_tmp[:, 1:2],
                                         1.0 / D)
                cx.edge(v, i2, "SV")
                i3 = v.tensor_mul(st_tmp[:, 3:4], st_tmp[:, 2:3],
                                  st_tmp[:, 2:3])
                cx.edge(v, i3, "SV")
                i4 = v.tensor_scalar_mul(st_tmp[:, 3:4], st_tmp[:, 3:4],
                                         float(D) / (D - 1))
                cx.edge(v, i4, "SV")
                cx.inc(v.scalar_tensor_tensor(st_tmp[:, 3:4], st_tmp[:, 0:1],
                                              1.0 / (D - 1), st_tmp[:, 3:4],
                                              ALU.mult, ALU.subtract), "SV")
            m_var2 = cx.cnt["SV"]

            @block.scalar
            def _(s_):
                cx.w(s_, "SV", m_var2)
                i1 = s_.activation(st_tmp[:, 4:5], st_tmp[:, 3:4], AF.Ln)
                cx.edge(s_, i1, "SS")
                cx.inc(s_.activation(st_tmp[:, 4:5], st_tmp[:, 4:5],
                                     AF.Exp, scale=0.5), "SS")
            m_sd = cx.cnt["SS"]

            @block.vector
            def _(v):
                cx.w(v, "SS", m_sd)
                i1 = v.tensor_scalar_add(st_tmp[:, 5:6], st_tmp[:, 4:5],
                                         1e-6)
                cx.edge(v, i1, "SV")
                i2 = v.reciprocal(st_tmp[:, 6:7], st_tmp[:, 5:6])
                cx.edge(v, i2, "SV")
                cx.inc(v.scalar_tensor_tensor(st_tmp[:, 7:8],
                                              st_tmp[:, 2:3], -1.0,
                                              st_tmp[:, 6:7],
                                              ALU.mult, ALU.mult), "SV")
            m_r2 = cx.cnt["SV"]

            @block.scalar
            def _(s_):
                cx.w(s_, "SV", m_r2)
                cx.inc(s_.activation(c_t1, c_v, AF.Identity,
                                     scale=st_tmp[:, 6:7],
                                     bias=st_tmp[:, 7:8]), "SS")
            m_n2 = cx.cnt["SS"]

            @block.vector
            def _(v):
                cx.w(v, "SS", m_n2)
                i1 = v.tensor_mul(c_v, c_t1, alpha_rep)
                cx.edge(v, i1, "SV")
                i2 = v.tensor_add(c_v, c_v, beta_rep)
                cx.edge(v, i2, "SV")
                cx.inc(v.tensor_add(c_v, c_v, xl), "SV")
            m_x1 = cx.cnt["SV"]

            @block.sync
            def _(q):
                cx.w(q, "SV", m_x1)
                cx.gdma(q, xres[gi * 128:(gi + 1) * 128, :], c_v, g_xr)

    if maybe_stop("c"):
        stack.close()
        return nc

    A.reset(0)
    f_w = A.alloc(128, 8 * 1024, F32)
    f_wd = A.alloc(128, 8 * 1024, BF16)
    q_u = A.alloc(128, 1024, F32)
    q_a = A.alloc(128, 1024, F32)
    q_m2 = A.alloc(128, 1024, F32)
    q_m4 = A.alloc(128, 1024, F32)
    q_f = A.alloc(128, 1024, F32)
    q_i = A.alloc(128, 1024, I32)
    q_r = A.alloc(128, 1024, F32)
    q_bm = A.alloc(128, 32, F32)
    q_sc = A.alloc(128, 32, F32)
    q_iv = A.alloc(128, 32, F32)
    f_x = [A.alloc(128, 1024, F32), A.alloc(128, 1024, F32)]
    f_xh = A.alloc(128, 1024, F32)
    f_xd = A.alloc(128, 1024, BF16)
    f_xdT = A.alloc(128, 1024, BF16)
    rowfb_bf = P.alloc(1, D, BF16)

    def bc3(small, big_cols=1024):
        return small[:, :, None].broadcast_to([128, 32, 32])

    def quantize(src_f32, dst_bf, tag):
        @block.vector
        def _(v):
            cx.wcur(v, "SS")
            i1 = v.tensor_reduce(q_bm,
                                 src_f32.rearrange("p (b e) -> p b e", e=32),
                                 AX.X, ALU.max, apply_absolute_value=True)
            cx.edge(v, i1, "SV")
            cx.inc(v.tensor_scalar(q_sc, q_bm, 1.0 / 7.5, 1e-30,
                                   ALU.mult, ALU.max), "SV")
        m1 = cx.cnt["SV"]

        @block.scalar
        def _(s_):
            cx.w(s_, "SV", m1)
            i1 = s_.activation(q_iv, q_sc, AF.Ln)
            cx.edge(s_, i1, "SS")
            cx.inc(s_.activation(q_iv, q_iv, AF.Exp, scale=-1.0), "SS")
        m2_ = cx.cnt["SS"]

        @block.vector
        def _(v):
            cx.w(v, "SS", m2_)
            cx.inc(v.tensor_tensor(
                q_u.rearrange("p (b e) -> p b e", e=32),
                src_f32.rearrange("p (b e) -> p b e", e=32),
                bc3(q_iv), ALU.mult), "SV")
        m3 = cx.cnt["SV"]

        @block.scalar
        def _(s_):
            cx.w(s_, "SV", m3)
            cx.inc(s_.activation(q_a, q_u, AF.Abs), "SS")
        m4_ = cx.cnt["SS"]

        @block.vector
        def _(v):
            cx.w(v, "SS", m4_)
            v.tensor_single_scalar(q_m2, q_a, 2.0, ALU.is_ge)
            v.tensor_single_scalar(q_m4, q_a, 4.0, ALU.is_ge)
            v.tensor_scalar(q_f, q_m2, -4.0, 8.0, ALU.mult, ALU.add)
            i1 = v.scalar_tensor_tensor(q_f, q_m4, -2.0, q_f,
                                        ALU.mult, ALU.add)
            cx.edge(v, i1, "SV")
            v.tensor_mul(q_r, q_u, q_f)
            v.tensor_copy(q_i, q_r)
            v.tensor_copy(q_r, q_i)
            v.tensor_scalar(q_f, q_m2, 0.125, 0.125, ALU.mult, ALU.add)
            i2 = v.scalar_tensor_tensor(q_f, q_m4, 0.25, q_f,
                                        ALU.mult, ALU.add)
            cx.edge(v, i2, "SV")
            i3 = v.tensor_mul(q_r, q_r, q_f)
            cx.edge(v, i3, "SV")
            cx.inc(v.tensor_tensor(
                dst_bf.rearrange("p (b e) -> p b e", e=32),
                q_r.rearrange("p (b e) -> p b e", e=32),
                bc3(q_sc), ALU.mult), "SV")
        return cx.cnt["SV"]

    g_fw = cx.group()

    @block.sync
    def _(q):
        cx.wcur(q, "SV", "SS", "ST", "SP")
        cx.gw(q, g_xr)
        for k in range(NKD):
            cx.gdma(q, f_w[:, k * 1024:(k + 1) * 1024],
                    fw[k * 128:(k + 1) * 128, :], g_fw)

    @block.gpsimd
    def _(q):
        cx.gdma(q, rowfb_bf, fb_r, g_fw)

    @block.vector
    def _(v):
        cx.gw(v, g_fw)
        cx.inc(v.tensor_copy(st_tmp[:, 8:9], st_tmp[:, 8:9]), "SV")
    m_fwl = cx.cnt["SV"]

    for jt in range(NKD):
        quantize(f_w[:, jt * 1024:(jt + 1) * 1024],
                 f_wd[:, jt * 1024:(jt + 1) * 1024], f"w{jt}")
    m_wq = cx.cnt["SV"]

    for kt in range(NKD):
        @block.tensor
        def _(t):
            cx.w(t, "SV", m_wq)
            if kt >= 1:
                cx.wcur(t, "SV")
            mm = None
            for jt in range(NKD):
                mm = t.transpose(PE[:, jt * 128:(jt + 1) * 128],
                                 f_wd[:, jt * 1024 + kt * 128:
                                      jt * 1024 + (kt + 1) * 128], identb)
            cx.inc(mm, "ST")
        m_tp = cx.cnt["ST"]

        @block.vector
        def _(v):
            cx.w(v, "ST", m_tp)
            cx.inc(v.tensor_copy(wdT_bf[:, kt * 1024:(kt + 1) * 1024],
                                 PE[:, :]), "SV")
    m_wdT = cx.cnt["SV"]

    def ffn_phase(tag, dst=None):
        if dst is None:
            dst = xres
        g_fx = [cx.group(), cx.group()]
        g_fo = cx.group()
        for tpo in range(16):
            xl = f_x[tpo % 2]

            @block.sync
            def _(q):
                cx.wcur(q, "SV", "SS")
                if tag == "f1":
                    cx.gw(q, g_xr)
                cx.gdma(q, xl, xres[tpo * 128:(tpo + 1) * 128, :],
                        g_fx[tpo % 2])

            @block.scalar
            def _(s_):
                cx.gw(s_, g_fx[tpo % 2])
                cx.wcur(s_, "SV")
                cx.inc(s_.activation(q_a, xl, AF.Square,
                                     accum_out=st_tmp[:, 0:1]), "SS")
            m_sq = cx.cnt["SS"]

            @block.vector
            def _(v):
                cx.w(v, "SS", m_sq)
                i1 = v.reduce_sum(st_tmp[:, 1:2], xl, AX.X)
                cx.edge(v, i1, "SV")
                i2 = v.tensor_scalar_mul(st_tmp[:, 2:3], st_tmp[:, 1:2],
                                         1.0 / D)
                cx.edge(v, i2, "SV")
                i3 = v.tensor_mul(st_tmp[:, 3:4], st_tmp[:, 2:3],
                                  st_tmp[:, 2:3])
                cx.edge(v, i3, "SV")
                cx.inc(v.scalar_tensor_tensor(st_tmp[:, 3:4],
                                              st_tmp[:, 0:1], 1.0 / D,
                                              st_tmp[:, 3:4],
                                              ALU.mult, ALU.subtract), "SV")
            m_va = cx.cnt["SV"]

            @block.scalar
            def _(s_):
                cx.w(s_, "SV", m_va)
                i1 = s_.activation(st_tmp[:, 4:5], st_tmp[:, 3:4], AF.Ln,
                                   bias=1e-5)
                cx.edge(s_, i1, "SS")
                cx.inc(s_.activation(st_tmp[:, 4:5], st_tmp[:, 4:5],
                                     AF.Exp, scale=-0.5), "SS")
            m_rs = cx.cnt["SS"]

            @block.vector
            def _(v):
                cx.w(v, "SS", m_rs)
                cx.inc(v.scalar_tensor_tensor(st_tmp[:, 5:6],
                                              st_tmp[:, 2:3], -1.0,
                                              st_tmp[:, 4:5],
                                              ALU.mult, ALU.mult), "SV")
            m_nbv = cx.cnt["SV"]

            @block.scalar
            def _(s_):
                cx.w(s_, "SV", m_nbv)
                cx.gw(s_, g_fo)
                cx.inc(s_.activation(f_xh, xl, AF.Identity,
                                     scale=st_tmp[:, 4:5],
                                     bias=st_tmp[:, 5:6]), "SS")

            quantize(f_xh, f_xd, tag + str(tpo))
            m_xq = cx.cnt["SV"]

            @block.tensor
            def _(t):
                cx.w(t, "SV", m_xq)
                cx.wcur(t, "SV")
                mm = None
                for k in range(NKD):
                    mm = t.transpose(PE[:, k * 128:(k + 1) * 128],
                                     f_xd[:, k * 128:(k + 1) * 128], identb)
                cx.inc(mm, "ST")
            m_tx = cx.cnt["ST"]

            @block.vector
            def _(v):
                cx.w(v, "ST", m_tx)
                cx.inc(v.tensor_copy(f_xdT, PE[:, :]), "SV")
            m_xdT = cx.cnt["SV"]

            @block.tensor
            def _(t):
                cx.w(t, "SV", m_xdT)
                if tpo >= 1:
                    cx.wcur(t, "SV")
                mm = None
                for k in range(NKD):
                    for f in range(2):
                        mm = t.matmul(PA[:, f * 512:(f + 1) * 512],
                                      f_xdT[:, k * 128:(k + 1) * 128],
                                      wdT_bf[:, k * 1024 + f * 512:
                                             k * 1024 + (f + 1) * 512],
                                      start=(k == 0), stop=False)
                for f in range(2):
                    mm = t.matmul(PA[:, f * 512:(f + 1) * 512],
                                  ones_bf,
                                  rowfb_bf[:, f * 512:(f + 1) * 512],
                                  start=False, stop=True)
                cx.inc(mm, "ST")
            m_fm = cx.cnt["ST"]

            @block.vector
            def _(v):
                cx.w(v, "ST", m_fm)
                cx.inc(v.tensor_add(f_xh, PA[:, :], xl), "SV")
            m_fx = cx.cnt["SV"]

            @block.sync
            def _(q):
                cx.w(q, "SV", m_fx)
                cx.gdma(q, dst[tpo * 128:(tpo + 1) * 128, :], f_xh, g_fo)
        return g_fo

    g_f1 = ffn_phase("f1")

    if maybe_stop("ffn1"):
        stack.close()
        return nc

    A.reset(0)
    ab_qf = [A.alloc(128, 2048, BF16) for _ in range(8)]
    ab_kf = [A.alloc(128, 2048, BF16) for _ in range(4)]
    ab_v = A.alloc(128, 16 * 256, BF16)
    ab_o = [A.alloc(128, 2048, BF16) for _ in range(8)]
    ab_wA = A.alloc(128, 8 * 1024, BF16)
    ab_srun = A.alloc(64, 264, F32)
    ab_stot = A.alloc(64, 264, F32)
    st_r3 = A.alloc(128, 16, F32)
    st_nb3 = A.alloc(128, 16, F32)
    ab_kfT = A.alloc(128, 64, BF16)
    ab_ck = A.top

    wp_wc2 = A.alloc(128, 128, F32)
    wp_wcT = A.alloc(64, 256, F32)
    wp_we = A.alloc(64, 256, F32)
    wp_comb = A.alloc(128, 512, F32)
    wp_wT = A.alloc(128, 2048, F32)
    wp_ld = A.alloc(128, 256, F32)
    sb_identf = A.alloc(128, 128, F32)

    rowboB_bf = P.alloc(1, D, BF16)
    rowbv2_bf = P.alloc(1, 256, BF16)
    ones2b = P.alloc(128, 128, BF16)
    sb_trimf = P.alloc(128, 128, F32)

    g_wp = cx.group()
    g_wq2 = cx.group()
    g_bk2 = cx.group()
    g_kf = cx.group()
    g_sAG = cx.group()
    g_sld = cx.group()
    g_srb = cx.group()
    g_ep = cx.group()
    g_xb = cx.group()
    g_xb2 = cx.group()
    g_xo = cx.group()

    @block.sync
    def _(q):
        cx.wcur(q, "SV", "SS", "ST", "SP")
        cx.gw(q, g_f1)
        cx.gdma(q, wp_wc2[:, 0:64], wc_i[0:128, :], g_wp)
        cx.gdma(q, wp_wc2[:, 64:128], wc_i[128:256, :], g_wp)
        cx.gdma(q, wp_we, we_i, g_wp)
        cx.gdma(q, sb_identf, ident, g_wp)
        cx.gdma(q, sb_trimf, trimask, g_wp)

    @block.gpsimd
    def _(q):
        cx.wcur(q, "SV", "SS", "ST", "SP")
        cx.gw(q, g_f1)
        for kt in range(NKD):
            cx.gdma(q, ab_wA[:, kt * 1024:(kt + 1) * 1024],
                    wq[kt * 128:(kt + 1) * 128, :], g_wq2)
        cx.gdma(q, rowboB_bf, bo_r, g_wq2)

    @block.vector
    def _(v):
        cx.wcur(v, "SS", "ST", "SP")
        cx.gw(v, g_f1)
        cx.inc(v.memset(ones2b, 1.0), "SV")

    @block.tensor
    def _(t):
        cx.gw(t, g_wp)
        cx.wcur(t, "SV", "SS", "SP")
        mm = None
        for mt in range(2):
            mm = t.transpose(PD[0:64, mt * 128:(mt + 1) * 128],
                             wp_wc2[:, mt * 64:(mt + 1) * 64], sb_identf)
        cx.inc(mm, "ST")
    m_wct = cx.cnt["ST"]

    @block.vector
    def _(v):
        cx.w(v, "ST", m_wct)
        cx.inc(v.tensor_copy(wp_wcT, PD[0:64, 0:256]), "SV")
    m_wcT = cx.cnt["SV"]

    @block.tensor
    def _(t):
        cx.w(t, "SV", m_wcT)
        mm = None
        for at in range(2):
            mm = t.matmul(PA[:, at * 256:(at + 1) * 256],
                          wp_wcT[:, at * 128:(at + 1) * 128], wp_we,
                          start=True, stop=True)
        cx.inc(mm, "ST")
    m_cbm = cx.cnt["ST"]

    @block.vector
    def _(v):
        cx.w(v, "ST", m_cbm)
        cx.inc(v.tensor_copy(wp_comb, PA[:, 0:512]), "SV")
    m_comb = cx.cnt["SV"]

    for wi, (wsrc, bcol, dst_bf) in enumerate(
            [(wk, sb_bkc, wkc_bf), (wv, sb_bvc, wvc_bf)]):
        g_l = cx.group()
        for kt in range(NKD):
            @block.sync
            def _(q):
                cx.wcur(q, "ST")
                cx.gdma(q, wp_ld, wsrc[kt * 128:(kt + 1) * 128, :], g_l)

            @block.tensor
            def _(t):
                cx.gw(t, g_l)
                cx.w(t, "SV", m_comb)
                cx.wcur(t, "SV")
                mm = None
                for mt in range(2):
                    mm = t.transpose(PD[0:128, mt * 128:(mt + 1) * 128],
                                     wp_ld[:, mt * 128:(mt + 1) * 128],
                                     sb_identf)
                cx.inc(mm, "ST")
            m_tw = cx.cnt["ST"]

            @block.vector
            def _(v):
                cx.w(v, "ST", m_tw)
                mm = None
                for mt in range(2):
                    mm = v.tensor_copy(wp_wT[:, mt * 1024 + kt * 128:
                                             mt * 1024 + (kt + 1) * 128],
                                       PD[0:128, mt * 128:(mt + 1) * 128])
                cx.inc(mm, "SV")
        m_wT = cx.cnt["SV"]

        for po in range(NKD):
            @block.tensor
            def _(t):
                cx.w(t, "SV", m_wT)
                cx.wcur(t, "SS")
                mm = None
                for mt in range(2):
                    mm = t.matmul(PA[:, 0:256],
                                  wp_wT[:, mt * 1024 + po * 128:
                                        mt * 1024 + (po + 1) * 128],
                                  wp_comb[:, mt * 256:(mt + 1) * 256],
                                  start=(mt == 0), stop=(mt == 1))
                cx.inc(mm, "ST")
            m_km = cx.cnt["ST"]

            @block.scalar
            def _(s_):
                cx.w(s_, "ST", m_km)
                cx.inc(s_.activation(dst_bf[:, po * 256:(po + 1) * 256],
                                     PA[:, 0:256], AF.Identity), "SS")
        @block.tensor
        def _(t):
            cx.wcur(t, "SS")
            mm = None
            for mt in range(2):
                mm = t.matmul(PD[0:1, 0:256], bcol[:, mt:mt + 1],
                              wp_comb[:, mt * 256:(mt + 1) * 256],
                              start=(mt == 0), stop=(mt == 1))
            cx.inc(mm, "ST")
        m_bm = cx.cnt["ST"]

        if wi == 0:
            @block.vector
            def _(v):
                cx.w(v, "ST", m_bm)
                cx.inc(v.tensor_copy(sb_bv2r, PD[0:1, 0:256]), "SV")
            m_b2 = cx.cnt["SV"]

            @block.sync
            def _(q):
                cx.w(q, "SV", m_b2)
                cx.gdma(q, rowscr[0:1, 0:256], sb_bv2r, g_bk2)
                cx.gw(q, g_bk2)
                rr = rowscr[0:1, 0:256].rearrange("o (b p) -> o b p", p=128)
                with nc.allow_non_contiguous_dma(reason="tiny 128x2 bias"):
                    cx.gdma(q, sb_bk2,
                            rr.rearrange("o b p -> (o p) b"), g_bk2)
        else:
            @block.vector
            def _(v):
                cx.w(v, "ST", m_bm)
                cx.gw(v, g_bk2)
                i1 = v.tensor_copy(sb_bv2r, PD[0:1, 0:256])
                cx.edge(v, i1, "SV")
                cx.inc(v.tensor_copy(rowbv2_bf, sb_bv2r), "SV")
    m_wpend = cx.cnt["ST"]

    A.reset(ab_ck)
    ab_xT = A.alloc(128, 8 * 512, BF16)
    ab_xhb = A.alloc(128, 1024, BF16)
    ab_xl = A.alloc(128, 1024, F32)
    ab_xh = A.alloc(128, 1024, F32)
    ab_e = A.alloc(128, 512, F32)
    ab_m1 = A.alloc(128, 512, F32)
    ab_kt = A.alloc(128, 512, BF16)

    for ch in range(4):
        for sub in range(4):
            tc = ch * 4 + sub

            @block.sync
            def _(q):
                cx.wcur(q, "SV", "SS")
                if tc == 0:
                    cx.wall(q, ("ST", m_wpend))
                cx.gdma(q, ab_xl, xres[tc * 128:(tc + 1) * 128, :], g_xb)

            @block.scalar
            def _(s_):
                cx.gw(s_, g_xb)
                cx.wcur(s_, "SV")
                cx.inc(s_.activation(ab_xh, ab_xl, AF.Square,
                                     accum_out=st_tmp[:, 0:1]), "SS")
            m_sq = cx.cnt["SS"]

            @block.vector
            def _(v):
                cx.w(v, "SS", m_sq)
                i1 = v.reduce_sum(st_tmp[:, 1:2], ab_xl, AX.X)
                cx.edge(v, i1, "SV")
                i2 = v.tensor_scalar_mul(st_tmp[:, 2:3], st_tmp[:, 1:2],
                                         1.0 / D)
                cx.edge(v, i2, "SV")
                i3 = v.tensor_mul(st_tmp[:, 3:4], st_tmp[:, 2:3],
                                  st_tmp[:, 2:3])
                cx.edge(v, i3, "SV")
                cx.inc(v.scalar_tensor_tensor(st_tmp[:, 3:4],
                                              st_tmp[:, 0:1], 1.0 / D,
                                              st_tmp[:, 3:4],
                                              ALU.mult, ALU.subtract), "SV")
            m_va = cx.cnt["SV"]

            @block.scalar
            def _(s_):
                cx.w(s_, "SV", m_va)
                i1 = s_.activation(st_r3[:, tc:tc + 1], st_tmp[:, 3:4],
                                   AF.Ln, bias=1e-5)
                cx.edge(s_, i1, "SS")
                cx.inc(s_.activation(st_r3[:, tc:tc + 1],
                                     st_r3[:, tc:tc + 1],
                                     AF.Exp, scale=-0.5), "SS")
            m_rs = cx.cnt["SS"]

            @block.vector
            def _(v):
                cx.w(v, "SS", m_rs)
                cx.inc(v.scalar_tensor_tensor(st_nb3[:, tc:tc + 1],
                                              st_tmp[:, 2:3], -1.0,
                                              st_r3[:, tc:tc + 1],
                                              ALU.mult, ALU.mult), "SV")
            m_nb3 = cx.cnt["SV"]

            @block.scalar
            def _(s_):
                cx.w(s_, "SV", m_nb3)
                cx.wcur(s_, "ST")
                cx.inc(s_.activation(ab_xhb, ab_xl, AF.Identity,
                                     scale=st_r3[:, tc:tc + 1],
                                     bias=st_nb3[:, tc:tc + 1]), "SS")
            m_h3 = cx.cnt["SS"]

            @block.tensor
            def _(t):
                cx.w(t, "SS", m_h3)
                cx.wcur(t, "SV")
                mm = None
                for kt in range(NKD):
                    mm = t.transpose(PE[:, kt * 128:(kt + 1) * 128],
                                     ab_xhb[:, kt * 128:(kt + 1) * 128],
                                     identb)
                cx.inc(mm, "ST")
            m_tr = cx.cnt["ST"]

            @block.vector
            def _(v):
                cx.w(v, "ST", m_tr)
                mm = None
                for kt in range(NKD):
                    mm = v.tensor_copy(ab_xT[:, kt * 512 + sub * 128:
                                             kt * 512 + (sub + 1) * 128],
                                       PE[:, kt * 128:(kt + 1) * 128])
                cx.inc(mm, "SV")

        for jb in range(8):
            @block.tensor
            def _(t):
                cx.wcur(t, "SV", "SS")
                cx.gw(t, g_wq2)
                mm = None
                for kt in range(NKD):
                    mm = t.matmul(PA[:, 0:512],
                                  ab_wA[:, kt * 1024 + jb * 128:
                                        kt * 1024 + (jb + 1) * 128],
                                  ab_xT[:, kt * 512:(kt + 1) * 512],
                                  start=(kt == 0), stop=(kt == NKD - 1))
                cx.inc(mm, "ST")
            m_qm = cx.cnt["ST"]

            @block.vector
            def _(v):
                cx.w(v, "ST", m_qm)
                i1 = v.tensor_scalar(ab_m1, PA[:, 0:512],
                                     sb_bqc[:, jb:jb + 1], 0.0,
                                     ALU.add, ALU.max)
                cx.edge(v, i1, "SV")
                cx.inc(v.tensor_scalar(ab_e, PA[:, 0:512],
                                       sb_bqc[:, jb:jb + 1], 0.0,
                                       ALU.add, ALU.min), "SV")
            m_q1 = cx.cnt["SV"]

            @block.scalar
            def _(s_):
                cx.w(s_, "SV", m_q1)
                cx.inc(s_.activation(ab_e, ab_e, AF.Exp), "SS")
            m_q2 = cx.cnt["SS"]

            @block.vector
            def _(v):
                cx.w(v, "SS", m_q2)
                cx.inc(v.tensor_add(ab_qf[jb][:, ch * 512:(ch + 1) * 512],
                                    ab_m1, ab_e), "SV")

        for cb in range(2):
            @block.tensor
            def _(t):
                cx.wcur(t, "SV", "SS")
                mm = None
                for kt in range(NKD):
                    mm = t.matmul(PB[:, 0:512],
                                  wkc_bf[:, kt * 256 + cb * 128:
                                         kt * 256 + (cb + 1) * 128],
                                  ab_xT[:, kt * 512:(kt + 1) * 512],
                                  start=(kt == 0), stop=(kt == NKD - 1))
                cx.inc(mm, "ST")
            m_kmv = cx.cnt["ST"]

            @block.vector
            def _(v):
                cx.w(v, "ST", m_kmv)
                if ch == 0 and cb == 0:
                    cx.gw(v, g_bk2)
                i1 = v.tensor_scalar(ab_m1, PB[:, 0:512],
                                     sb_bk2[:, cb:cb + 1], 0.0,
                                     ALU.add, ALU.max)
                cx.edge(v, i1, "SV")
                cx.inc(v.tensor_scalar(ab_e, PB[:, 0:512],
                                       sb_bk2[:, cb:cb + 1], 0.0,
                                       ALU.add, ALU.min), "SV")
            m_k1 = cx.cnt["SV"]

            @block.scalar
            def _(s_):
                cx.w(s_, "SV", m_k1)
                cx.inc(s_.activation(ab_e, ab_e, AF.Exp), "SS")
            m_k2 = cx.cnt["SS"]

            @block.vector
            def _(v):
                cx.w(v, "SS", m_k2)
                cx.gw(v, g_kf)
                cx.inc(v.tensor_add(ab_kt, ab_m1, ab_e), "SV")
            m_k3 = cx.cnt["SV"]

            @block.sync
            def _(q):
                cx.w(q, "SV", m_k3)
                for g2 in range(2):
                    for half in range(2):
                        cx.gdma(q, ab_kf[2 * cb + g2][half * 64:
                                                      half * 64 + 64,
                                                      ch * 512:
                                                      (ch + 1) * 512],
                                ab_kt[g2 * 64:(g2 + 1) * 64, :], g_kf)

        for sub in range(4):
            tc = ch * 4 + sub

            @block.tensor
            def _(t):
                cx.wcur(t, "SV", "SS")
                mm = None
                for kt in range(NKD):
                    mm = t.matmul(PD[:, 0:256],
                                  ab_xT[:, kt * 512 + sub * 128:
                                        kt * 512 + (sub + 1) * 128],
                                  wvc_bf[:, kt * 256:(kt + 1) * 256],
                                  start=(kt == 0), stop=False)
                mm = t.matmul(PD[:, 0:256], ones_bf, rowbv2_bf,
                              start=False, stop=True)
                cx.inc(mm, "ST")
            m_vm = cx.cnt["ST"]

            @block.vector
            def _(v):
                cx.w(v, "ST", m_vm)
                cx.inc(v.tensor_copy(ab_v[:, tc * 256:(tc + 1) * 256],
                                     PD[:, 0:256]), "SV")

        for sub in range(4):
            tc = ch * 4 + sub
            for g2 in range(4):
                @block.tensor
                def _(t):
                    cx.wcur(t, "SV")
                    if g2 == 0:
                        cx.gw(t, g_kf)
                    mm = t.transpose(PE[:, 0:64],
                                     ab_kf[g2][0:64,
                                               tc * 128:(tc + 1) * 128],
                                     identb[0:64, 0:64])
                    cx.inc(mm, "ST")
                m_kt2 = cx.cnt["ST"]

                @block.vector
                def _(v):
                    cx.w(v, "ST", m_kt2)
                    cx.inc(v.tensor_copy(ab_kfT, PE[:, 0:64]), "SV")
                m_kfT = cx.cnt["SV"]

                @block.tensor
                def _(t):
                    cx.w(t, "SV", m_kfT)
                    cx.inc(t.matmul(PC[0:64, g2 * 64:(g2 + 1) * 64],
                                    ab_kfT,
                                    ab_v[:, tc * 256 + g2 * 64:
                                         tc * 256 + (g2 + 1) * 64],
                                    start=(tc == 0), stop=(tc == 15)),
                           "ST")
    m_S = cx.cnt["ST"]

    @block.vector
    def _(v):
        cx.w(v, "ST", m_S)
        cx.gw(v, g_kf)
        i1 = v.tensor_copy(ab_stot[:, 0:256], PC[0:64, 0:256])
        cx.edge(v, i1, "SV")
        mm = None
        for g2 in range(4):
            mm = v.reduce_sum(ab_stot[:, 256 + g2:257 + g2],
                              ab_kf[g2][0:64, :], AX.X)
        cx.edge(v, mm, "SV")
        cx.inc(v.memset(ab_stot[:, 260:264], 0.0), "SV")
    m_pack = cx.cnt["SV"]

    @block.sync
    def _(q):
        cx.w(q, "SV", m_pack)
        cx.gdma(q, ags_in, ab_stot, g_sAG)
        cx.wall(q, ("ST", m_S))

    @block.gpsimd
    def _(q):
        cx.gw(q, g_sAG)
        cx.inc(q.collective_compute("AllGather", ALU.bypass,
                                    replica_groups=RG,
                                    ins=[ags_in.opt()],
                                    outs=[ags_out.opt()]), "SC")
        for kt in range(NKD):
            cx.gdma(q, ab_wA[:, kt * 1024:(kt + 1) * 1024],
                    wo[kt * 128:(kt + 1) * 128, :], g_wq2)
    m_agS = cx.cnt["SC"]

    A.reset(ab_ck)
    ab_am = A.alloc(128, 256, BF16)
    ab_sr0 = A.alloc(64, 260, BF16)
    ab_srbf2 = A.alloc(128, 260, BF16)
    ab_krb = A.alloc(128, 512, BF16)
    ab_rden = A.alloc(128, 512, F32)
    ab_rrep = A.alloc(128, 128, BF16)
    ab_sg = A.alloc(64, 264, F32)
    ab_kc2 = A.alloc(128, 4, F32)
    ab_tp = A.alloc(128, 256, F32)
    epatf = A.alloc(128, 1024, F32)

    @block.sync
    def _(q):
        cx.wall(q, ("ST", m_S))
        cx.wcur(q, "SV")
        cx.gdma(q, epatf, epat, g_ep)

    for j in range(W):
        @block.sync
        def _(q):
            if j == 0:
                cx.w(q, "SC", m_agS)
            cx.wcur(q, "SV")
            cx.gdma(q, ab_sg, ags_out[j * 64:(j + 1) * 64, :], g_sld)

        @block.vector
        def _(v):
            cx.gw(v, g_sld)
            if j == 0:
                cx.inc(v.tensor_scalar_mul(ab_srun, ab_sg,
                                           sb_pmask[:, 0:1]), "SV")
            else:
                cx.inc(v.scalar_tensor_tensor(ab_srun, ab_sg,
                                              sb_pmask[:, j:j + 1],
                                              ab_srun,
                                              ALU.mult, ALU.add), "SV")

    for tc in range(16):
        @block.vector
        def _(v):
            cx.wcur(v, "ST")
            cx.gw(v, g_srb)
            cx.inc(v.tensor_copy(ab_sr0, ab_srun[:, 0:260]), "SV")
        m_sr0 = cx.cnt["SV"]

        @block.sync
        def _(q):
            cx.w(q, "SV", m_sr0)
            for half in range(2):
                cx.gdma(q, ab_srbf2[half * 64:half * 64 + 64, :],
                        ab_sr0, g_srb)
                cx.gdma(q, ab_kc2[half * 64:half * 64 + 64, :],
                        ab_srun[:, 256:260], g_srb)

        @block.vector
        def _(v):
            cx.gw(v, g_srb)
            mm = None
            for g2 in range(4):
                mm = v.tensor_scalar_mul(
                    ab_krb[:, g2 * 128:(g2 + 1) * 128], ones2b,
                    ab_kc2[:, g2:g2 + 1])
            cx.inc(mm, "SV")
        m_rep = cx.cnt["SV"]

        for jb in range(8):
            for sub in range(2):
                h = 2 * jb + sub
                qb = 64 * sub
                g2 = h // 4

                @block.tensor
                def _(t):
                    cx.wall(t, ("SV", m_rep))
                    cx.wcur(t, "SV")
                    cx.inc(t.matmul(PD[:, sub * 128:(sub + 1) * 128],
                                    ab_kf[g2][qb:qb + 64,
                                              tc * 128:(tc + 1) * 128],
                                    ab_qf[jb][qb:qb + 64,
                                              tc * 128:(tc + 1) * 128],
                                    start=True, stop=True), "ST")
                m_at = cx.cnt["ST"]

                @block.vector
                def _(v):
                    cx.w(v, "ST", m_at)
                    cx.inc(v.tensor_tensor(
                        ab_am[:, sub * 128:(sub + 1) * 128],
                        PD[:, sub * 128:(sub + 1) * 128],
                        sb_trimf, ALU.mult), "SV")
                m_am = cx.cnt["SV"]

                @block.tensor
                def _(t):
                    cx.w(t, "SV", m_am)
                    t.matmul(PC[qb:qb + 64, 0:128],
                             ab_v[:, tc * 256 + g2 * 64:
                                  tc * 256 + (g2 + 1) * 64],
                             ab_am[:, sub * 128:(sub + 1) * 128],
                             start=True, stop=True)
                    t.matmul(PC[qb:qb + 64, 128:256],
                             ab_srbf2[qb:qb + 64, g2 * 64:(g2 + 1) * 64],
                             ab_qf[jb][qb:qb + 64,
                                       tc * 128:(tc + 1) * 128],
                             start=True, stop=True)
                    dl = 512 + sub * 256
                    t.matmul(PC[:, dl:dl + 128], ones2b,
                             ab_am[:, sub * 128:(sub + 1) * 128],
                             start=True, stop=True)
                    cx.inc(t.matmul(PC[:, dl + 128:dl + 256],
                                    ab_krb[qb:qb + 64,
                                           g2 * 128:(g2 + 1) * 128],
                                    ab_qf[jb][qb:qb + 64,
                                              tc * 128:(tc + 1) * 128],
                                    start=True, stop=True), "ST")
                m_nd = cx.cnt["ST"]

                @block.vector
                def _(v):
                    cx.w(v, "ST", m_nd)
                    m4 = h % 4
                    dl = 512 + sub * 256
                    i1 = v.tensor_copy(
                        ab_tp[32 * m4:32 * m4 + 32, 128:256],
                        PC[32 * m4:32 * m4 + 32, dl:dl + 128])
                    cx.edge(v, i1, "SV")
                    cx.inc(v.tensor_tensor(
                        ab_rden[32 * m4:32 * m4 + 32,
                                (h // 4) * 128:(h // 4 + 1) * 128],
                        ab_tp[32 * m4:32 * m4 + 32, 128:256],
                        PC[32 * m4:32 * m4 + 32, dl + 128:dl + 256],
                        ALU.add), "SV")
                    if sub == 1:
                        i2 = v.tensor_copy(ab_tp[:, 0:128],
                                           PC[0:128, 0:128])
                        cx.edge(v, i2, "SV")
                        cx.inc(v.tensor_tensor(
                            ab_o[jb][:, tc * 128:(tc + 1) * 128],
                            ab_tp[:, 0:128], PC[0:128, 128:256],
                            ALU.add), "SV")
        m_dall = cx.cnt["SV"]

        @block.scalar
        def _(s_):
            cx.w(s_, "SV", m_dall)
            i1 = s_.activation(ab_rden, ab_rden, AF.Ln, bias=1e-6)
            cx.edge(s_, i1, "SS")
            cx.inc(s_.activation(ab_rden, ab_rden, AF.Exp, scale=-1.0),
                   "SS")
        m_rd = cx.cnt["SS"]

        for jb in range(8):
            h0, h1 = 2 * jb, 2 * jb + 1
            b0 = (h0 % 4) * 2
            b1 = (h1 % 4) * 2 + 1

            @block.tensor
            def _(t):
                cx.w(t, "SS", m_rd)
                cx.wcur(t, "SV")
                if jb == 0:
                    cx.gw(t, g_ep)
                t.matmul(PA[:, 512:640],
                         epatf[:, b0 * 128:(b0 + 1) * 128],
                         ab_rden[:, (jb // 2) * 128:(jb // 2 + 1) * 128],
                         start=True, stop=False)
                cx.inc(t.matmul(PA[:, 512:640],
                                epatf[:, b1 * 128:(b1 + 1) * 128],
                                ab_rden[:, (jb // 2) * 128:
                                        (jb // 2 + 1) * 128],
                                start=False, stop=True), "ST")
            m_re = cx.cnt["ST"]

            @block.vector
            def _(v):
                cx.w(v, "ST", m_re)
                i1 = v.tensor_copy(ab_rrep, PA[:, 512:640])
                cx.edge(v, i1, "SV")
                cx.inc(v.tensor_mul(ab_o[jb][:, tc * 128:(tc + 1) * 128],
                                    ab_o[jb][:, tc * 128:(tc + 1) * 128],
                                    ab_rrep), "SV")

        for g2 in range(4):
            @block.tensor
            def _(t):
                cx.wcur(t, "SV")
                mm = t.transpose(PE[:, 0:64],
                                 ab_kf[g2][0:64, tc * 128:(tc + 1) * 128],
                                 identb[0:64, 0:64])
                cx.inc(mm, "ST")
            m_kt3 = cx.cnt["ST"]

            @block.vector
            def _(v):
                cx.w(v, "ST", m_kt3)
                cx.inc(v.tensor_copy(ab_kfT, PE[:, 0:64]), "SV")
            m_kfT3 = cx.cnt["SV"]

            @block.tensor
            def _(t):
                cx.w(t, "SV", m_kfT3)
                cx.inc(t.matmul(PB[0:64, g2 * 64:(g2 + 1) * 64], ab_kfT,
                                ab_v[:, tc * 256 + g2 * 64:
                                     tc * 256 + (g2 + 1) * 64],
                                start=True, stop=True), "ST")
        m_ds = cx.cnt["ST"]

        @block.vector
        def _(v):
            cx.w(v, "ST", m_ds)
            i1 = v.tensor_copy(ab_sg[:, 0:256], PB[0:64, 0:256])
            cx.edge(v, i1, "SV")
            mm = None
            for g2 in range(4):
                mm = v.reduce_sum(ab_sg[:, 256 + g2:257 + g2],
                                  ab_kf[g2][0:64, tc * 128:(tc + 1) * 128],
                                  AX.X)
            cx.edge(v, mm, "SV")
            cx.inc(v.tensor_add(ab_srun[:, 0:260], ab_srun[:, 0:260],
                                ab_sg[:, 0:260]), "SV")

    A.reset(ab_ck)
    ao_xl = A.alloc(128, 1024, F32)
    ao_xh = A.alloc(128, 1024, F32)
    ao_v2 = A.alloc(128, 1024, F32)
    ao_t1 = A.alloc(128, 1024, F32)

    for tc in range(16):
        @block.sync
        def _(q):
            cx.wcur(q, "SV", "SS", "ST")
            cx.gdma(q, ao_xl, xres[tc * 128:(tc + 1) * 128, :], g_xb2)

        @block.tensor
        def _(t):
            cx.wcur(t, "SV")
            cx.gw(t, g_wq2)
            mm = None
            for jb in range(8):
                for f in range(2):
                    mm = t.matmul(PA[:, f * 512:(f + 1) * 512],
                                  ab_o[jb][:, tc * 128:(tc + 1) * 128],
                                  ab_wA[:, jb * 1024 + f * 512:
                                        jb * 1024 + (f + 1) * 512],
                                  start=(jb == 0), stop=False)
            for f in range(2):
                mm = t.matmul(PA[:, f * 512:(f + 1) * 512], ones_bf,
                              rowboB_bf[:, f * 512:(f + 1) * 512],
                              start=False, stop=True)
            cx.inc(mm, "ST")
        m_om = cx.cnt["ST"]

        @block.scalar
        def _(s_):
            cx.gw(s_, g_xb2)
            cx.wcur(s_, "SV")
            cx.inc(s_.activation(ao_xh, ao_xl, AF.Identity,
                                 scale=st_r3[:, tc:tc + 1],
                                 bias=st_nb3[:, tc:tc + 1]), "SS")
        m_x3 = cx.cnt["SS"]

        @block.vector
        def _(v):
            cx.wall(v, ("ST", m_om), ("SS", m_x3))
            cx.inc(v.tensor_add(ao_v2, PA[:, :], ao_xh), "SV")
        m_v2 = cx.cnt["SV"]

        @block.scalar
        def _(s_):
            cx.w(s_, "SV", m_v2)
            cx.inc(s_.activation(ao_t1, ao_v2, AF.Square,
                                 accum_out=st_tmp[:, 0:1]), "SS")
        m_q2b = cx.cnt["SS"]

        @block.vector
        def _(v):
            cx.w(v, "SS", m_q2b)
            i1 = v.reduce_sum(st_tmp[:, 1:2], ao_v2, AX.X)
            cx.edge(v, i1, "SV")
            i2 = v.tensor_scalar_mul(st_tmp[:, 2:3], st_tmp[:, 1:2],
                                     1.0 / D)
            cx.edge(v, i2, "SV")
            i3 = v.tensor_mul(st_tmp[:, 3:4], st_tmp[:, 2:3],
                              st_tmp[:, 2:3])
            cx.edge(v, i3, "SV")
            cx.inc(v.scalar_tensor_tensor(st_tmp[:, 3:4], st_tmp[:, 0:1],
                                          1.0 / D, st_tmp[:, 3:4],
                                          ALU.mult, ALU.subtract), "SV")
        m_vb = cx.cnt["SV"]

        @block.scalar
        def _(s_):
            cx.w(s_, "SV", m_vb)
            i1 = s_.activation(st_tmp[:, 4:5], st_tmp[:, 3:4], AF.Ln,
                               bias=1e-5)
            cx.edge(s_, i1, "SS")
            cx.inc(s_.activation(st_tmp[:, 4:5], st_tmp[:, 4:5], AF.Exp,
                                 scale=-0.5), "SS")
        m_rb = cx.cnt["SS"]

        @block.vector
        def _(v):
            cx.w(v, "SS", m_rb)
            cx.inc(v.scalar_tensor_tensor(st_tmp[:, 5:6], st_tmp[:, 2:3],
                                          -1.0, st_tmp[:, 4:5],
                                          ALU.mult, ALU.mult), "SV")
        m_nbb = cx.cnt["SV"]

        @block.scalar
        def _(s_):
            cx.w(s_, "SV", m_nbb)
            cx.inc(s_.activation(ao_t1, ao_v2, AF.Identity,
                                 scale=st_tmp[:, 4:5],
                                 bias=st_tmp[:, 5:6]), "SS")
        m_ap = cx.cnt["SS"]

        @block.vector
        def _(v):
            cx.w(v, "SS", m_ap)
            i1 = v.tensor_mul(ao_v2, ao_t1, gamma_rep)
            cx.edge(v, i1, "SV")
            i2 = v.tensor_add(ao_v2, ao_v2, betab_rep)
            cx.edge(v, i2, "SV")
            cx.inc(v.tensor_add(ao_v2, ao_v2, ao_xl), "SV")
        m_x3o = cx.cnt["SV"]

        @block.sync
        def _(q):
            cx.w(q, "SV", m_x3o)
            cx.gdma(q, xres[tc * 128:(tc + 1) * 128, :], ao_v2, g_xo)

    if maybe_stop("attnb"):
        stack.close()
        return nc

    @block.vector
    def _(v):
        cx.wcur(v, "SS", "ST", "SP")
        cx.gw(v, g_xo)
        cx.inc(v.memset(st_tmp[:, 6:7], 0.0), "SV")

    g_f2 = ffn_phase("f2", dst=out_t)

    @block.sync
    def _(q):
        cx.gw(q, g_f2)
        cx.wcur(q, "SS", "SV", "ST", "SP", "SC")

    stack.close()
    return nc


def _host_inputs(inputs):
    x = np.asarray(inputs["L0_attn_a_x"], np.float32).reshape(16384, D)
    ins = []
    b_in_c = npf(inputs["L0_attn_a_b_in"]).reshape(32, 128).T
    conv_w = npf(inputs["L0_attn_a_conv_w"]).reshape(DI, 4)
    conv_w_c = conv_w.reshape(16, 128, 4).transpose(1, 0, 2).reshape(128, 64)
    conv_b_c = npf(inputs["L0_attn_a_conv_b"]).reshape(16, 128).T
    b_dt_c = npf(inputs["L0_attn_a_b_dt"]).reshape(16, 128).T
    d_skip_c = npf(inputs["L0_attn_a_D_skip"]).reshape(16, 128).T
    b_xp_c = npf(inputs["L0_attn_a_b_xp"]).reshape(32, 1)
    bq_c = npf(inputs["L0_attn_b_bq"]).reshape(8, 128).T
    bk_c = npf(inputs["L0_attn_b_bk"]).reshape(2, 128).T
    bv_c = npf(inputs["L0_attn_b_bv"]).reshape(2, 128).T
    ident = np.eye(128, dtype=np.float32)
    trimask = np.triu(np.ones((128, 128), np.float32), 0)
    ones_r = np.ones((1, 128), np.float32)
    patsel = np.zeros((16, 16 * 128), np.float32)
    for s in range(16):
        patsel[s, s * 128:(s + 1) * 128] = 1.0
    epat = np.zeros((128, 8 * 128), np.float32)
    for m in range(4):
        for half in range(2):
            blkc = (m * 2 + half) * 128
            epat[32 * m, blkc + 64 * half: blkc + 64 * half + 64] = 1.0
    A_log = npf(inputs["L0_attn_a_A_log"])

    shared = dict(
        w_in=npf(inputs["L0_attn_a_W_in"]), b_in_c=b_in_c,
        conv_w_c=conv_w_c, conv_b_c=conv_b_c,
        w_xp=npf(inputs["L0_attn_a_W_xp"]), b_xp_c=b_xp_c,
        w_dt=npf(inputs["L0_attn_a_W_dt"]), b_dt_c=b_dt_c,
        d_skip_c=d_skip_c,
        w_out=npf(inputs["L0_attn_a_W_out"]),
        b_out_r=npf(inputs["L0_attn_a_b_out"]).reshape(1, D),
        alpha_r=npf(inputs["L0_attn_a_alpha"]).reshape(1, D),
        beta_r=npf(inputs["L0_attn_a_beta"]).reshape(1, D),
        wq=npf(inputs["L0_attn_b_Wq"]), bq_c=bq_c,
        wk=npf(inputs["L0_attn_b_Wk"]), bk_c=bk_c,
        wv=npf(inputs["L0_attn_b_Wv"]), bv_c=bv_c,
        wc=npf(inputs["L0_attn_b_Wc"]), we=npf(inputs["L0_attn_b_We"]),
        wo=npf(inputs["L0_attn_b_Wo"]),
        bo_r=npf(inputs["L0_attn_b_bo"]).reshape(1, D),
        gamma_r=npf(inputs["L0_attn_b_gamma"]).reshape(1, D),
        beta_b_r=npf(inputs["L0_attn_b_beta"]).reshape(1, D),
        fw=npf(inputs["L0_ffn_weight"]),
        fb_r=npf(inputs["L0_ffn_bias"]).reshape(1, D),
        ident=ident, trimask=trimask, ones_r=ones_r,
        patsel=patsel, epat=epat,
        cconst=np.repeat(np.array([[1e-5, 1e-6, 1.0, -1.0, 0.5, -0.5,
                                    0.0, 2.0]], np.float32), 128, axis=0),
    )
    p = np.arange(128, dtype=np.int32)
    for r in range(W):
        percore = dict(shared)
        percore["x"] = np.ascontiguousarray(x[r * TL:(r + 1) * TL])
        h = np.zeros((128, D), np.float32)
        if r > 0:
            h[0:3, :] = x[r * TL - 3:r * TL]
        percore["xh3"] = h
        percore["b_in_hc"] = (b_in_c[:, :16] if r > 0
                              else np.zeros((128, 16), np.float32))
        ac = np.zeros((128, 32), np.float32)
        for b in range(2):
            ac[:, b * 16:(b + 1) * 16] = A_log[256 * r + 128 * b:
                                               256 * r + 128 * (b + 1), :]
        percore["a_cols"] = ac
        gdu = np.zeros((128, 16), np.int32)
        for b in range(2):
            for j in range(8):
                gdu[:, b * 8 + j] = 2 * (j * DI + 256 * r + 128 * b + p)
        percore["gidx_du"] = gdu
        gy = np.zeros((128, 16), np.int32)
        for m in range(16):
            gy[:, m] = 2 * ((m // 2) * TL + r * 256 + (m % 2) * 128 + p)
        percore["gidx_y"] = gy
        pm = np.zeros((64, 8), np.float32)
        pm[:, :r] = 1.0
        percore["pmask"] = pm
        ins.append(percore)
    return ins


def _run(inputs):
    ins = _host_inputs(inputs)
    nc = build_program()
    res = run_bass_kernel_spmd(nc, ins, core_ids=list(range(W)))
    return res.results


def kernel(**inputs):
    results = _run(inputs)
    out = np.concatenate([np.asarray(r["out"], np.float32)
                          for r in results], axis=0)
    return out.reshape(1, 16384, D)
